# version 02 (final)
# baseline (speedup 1.0000x reference)
import math
import numpy as np
import ml_dtypes

import concourse.bass as bass
import concourse.mybir as mybir
from concourse.bass_utils import run_bass_kernel_spmd

F32 = mybir.dt.float32
BF16 = mybir.dt.bfloat16
AF = mybir.ActivationFunctionType
ALU = mybir.AluOpType
AX = mybir.AxisListType
BFNP = ml_dtypes.bfloat16

NCORE = 8
B_, T_, D = 2, 8192, 1024
H, DH = 16, 64
TOK = 2048
TC = 512
NTC = TOK // TC
CH = 128
NCH = TOK // CH
MTC = 1024
NMTC = TOK // MTC
DFF = 2048
EH = 512
EPS = 1e-5
SCALE = 1.0 / math.sqrt(DH)

GAMMAS = (1.0 - np.exp(np.linspace(math.log(1 / 32.0), math.log(1 / 512.0), H))).astype(np.float64)

IN_PLACE_OK = True


ENGINES = ("tensor", "vector", "scalar", "gpsimd", "sync")
DMA_SEM_K = 8
DRAIN_ENGINES = ("vector", "scalar", "gpsimd")


class _Op:
    __slots__ = ("eng", "fn", "reads", "writes", "idx", "eidx", "dma_q", "cc_sem",
                 "deps", "need_inc", "waits", "drain", "tag", "cum_inc", "cum_dma",
                 "dma_key", "dma_val")


class EM:
    def __init__(self):
        self.ops = []
        self.eops = {e: [] for e in ENGINES}
        self.res = {}
        self.n_cc = 0
        self.marks = {}

    def mark(self, name):
        self.marks[name] = len(self.ops)

    def truncate(self, name):
        n = int(name[1:]) if isinstance(name, str) and name.startswith("@") else self.marks[name]
        self.ops = self.ops[:n]
        self.eops = {e: [o for o in self.ops if o.eng == e] for e in ENGINES}
        self.n_cc = sum(1 for o in self.ops if o.cc_sem is not None)
        res2 = {}
        for k, st in self.res.items():
            w = st["w"] if (st["w"] is not None and st["w"].idx < n) else None
            r = [o for o in st["r"] if o.idx < n]
            if w is not None or r:
                res2[k] = {"w": w, "r": r}
        self.res = res2

    def op(self, eng, fn, reads=(), writes=(), dma_q=None, cc=False, drain=False, tag=""):
        o = _Op()
        o.eng = eng; o.fn = fn; o.tag = tag
        o.reads = tuple(reads); o.writes = tuple(writes)
        o.idx = len(self.ops); o.eidx = len(self.eops[eng])
        o.dma_q = dma_q
        o.cc_sem = None
        o.deps = []; o.need_inc = False; o.waits = []; o.drain = drain
        if cc:
            o.cc_sem = self.n_cc
            self.n_cc += 1
        self.ops.append(o); self.eops[eng].append(o)
        deps = []
        for r in o.reads:
            st = self.res.setdefault(r, {"w": None, "r": []})
            if st["w"] is not None:
                deps.append(st["w"])
            st["r"].append(o)
        for w in o.writes:
            st = self.res.setdefault(w, {"w": None, "r": []})
            if st["w"] is not None:
                deps.append(st["w"])
            last = {}
            for rd in st["r"]:
                key = ("q", rd.dma_q) if rd.dma_q is not None else rd.eng
                if key not in last or rd.idx > last[key].idx:
                    last[key] = rd
            deps.extend(last.values())
            st["w"] = o; st["r"] = []
        seen = set()
        for d in deps:
            if d is not o and id(d) not in seen:
                seen.add(id(d)); o.deps.append(d)
        return o

    def barrier(self):
        toks = []
        for e in ENGINES:
            if self.eops[e] and self.eops[e][-1].tag != "barr":
                o = self.eops[e][-1]
                t = "@bar%d/%s" % (len(self.ops), e)
                self.res[t] = {"w": o, "r": []}
                toks.append(t)
        toks = tuple(toks)
        for e in ENGINES:
            self.op(e, lambda eng: eng.nop(), reads=toks, tag="barr")

    def finalize(self):
        import os
        if os.environ.get("KSAFE"):
            return self._finalize_serial()
        if os.environ.get("KAUG"):
            self._finalize_normal()
            return self._augment(os.environ["KAUG"])
        return self._finalize_normal()

    def _augment(self, mode):
        if "r" in mode:
            for o in self.ops:
                if o.eng in DRAIN_ENGINES and o.dma_q is None:
                    o.drain = True
        if mode == "r":
            return
        for e in ENGINES:
            c = 0
            for o in self.eops[e]:
                if o.dma_q is None and o.cc_sem is None:
                    o.need_inc = True
                    c += 1
                o.cum_inc = c
        engc = {e: 0 for e in ENGINES}
        dmac = {}
        for o in self.ops:
            extra = {}
            if "d" in mode and o.dma_q is None:
                for k, v in dmac.items():
                    extra[k] = v
            if "e" in mode and o.dma_q is None and o.cc_sem is None:
                for e in ENGINES:
                    if e != o.eng and engc[e] > 0:
                        extra[("eng", e)] = engc[e]
            cur = dict(o.waits)
            for k, v in extra.items():
                if cur.get(k, 0) < v:
                    cur[k] = v
            o.waits = sorted(cur.items(), key=lambda kv: str(kv[0]))
            if o.dma_q is not None:
                dmac[o.dma_key] = o.dma_val
            elif o.cc_sem is None:
                engc[o.eng] += 1

    def _finalize_normal(self):
        last_drain = {e: -1 for e in ENGINES}
        for o in self.ops:
            if o.drain and o.eng in DRAIN_ENGINES:
                last_drain[o.eng] = o.eidx - 1
            for d in o.deps:
                if d.eng == o.eng:
                    if d.dma_q is not None and o.dma_q is not None:
                        continue
                    if d.dma_q is not None or d.cc_sem is not None:
                        d.need_inc = True
                    elif o.eng in DRAIN_ENGINES:
                        if d.eidx > last_drain[o.eng]:
                            o.drain = True
                            last_drain[o.eng] = o.eidx - 1
                else:
                    d.need_inc = True
        dma_idx = {}
        for o in self.ops:
            if o.dma_q is not None:
                i = dma_idx.get(o.dma_q, 0)
                dma_idx[o.dma_q] = i + 1
                o.dma_key = ("dma", o.dma_q, i % DMA_SEM_K)
                o.dma_val = 16 * (i // DMA_SEM_K + 1)
        for e in ENGINES:
            c = 0
            for o in self.eops[e]:
                if o.need_inc and o.dma_q is None and o.cc_sem is None:
                    c += 1
                o.cum_inc = c
        maxw = {e: {} for e in ENGINES}
        for o in self.ops:
            req = {}
            if o.dma_q is not None and o.dma_val > 16:
                req[o.dma_key] = o.dma_val - 16
            for d in o.deps:
                if d.cc_sem is not None:
                    key, val = ("cc", d.cc_sem), 1
                elif d.dma_q is not None:
                    key, val = d.dma_key, d.dma_val
                elif d.eng != o.eng:
                    key, val = ("eng", d.eng), d.cum_inc
                else:
                    continue
                if req.get(key, -1) < val:
                    req[key] = val
            mw = maxw[o.eng]
            for key in sorted(req, key=str):
                val = req[key]
                if mw.get(key, 0) < val:
                    mw[key] = val
                    o.waits.append((key, val))

    def _finalize_serial(self):
        cnt = {("eng", e): 0 for e in ENGINES}
        dma_cnt = {}
        dma_idx = {}
        for o in self.ops:
            o.waits = []
            o.drain = o.eng in DRAIN_ENGINES
            for key, val in list(cnt.items()) + list(dma_cnt.items()):
                if val > 0 and key != ("eng", o.eng):
                    o.waits.append((key, val))
            if o.dma_q is not None:
                i = dma_idx.get(o.dma_q, 0)
                dma_idx[o.dma_q] = i + 1
                o.dma_key = ("dma", o.dma_q, i % DMA_SEM_K)
                o.dma_val = 16 * (i // DMA_SEM_K + 1)
                dma_cnt[o.dma_key] = o.dma_val
                o.need_inc = False
            elif o.cc_sem is not None:
                cnt[("cc", o.cc_sem)] = 1
            else:
                o.need_inc = True
                cnt[("eng", o.eng)] += 1
                o.cum_inc = cnt[("eng", o.eng)]
        for o in self.ops:
            pass

    def check(self):
        ptr = {e: 0 for e in ENGINES}
        cnt = {}
        progress = True
        total = sum(len(v) for v in self.eops.values())
        done = 0
        while progress:
            progress = False
            for e in ENGINES:
                while ptr[e] < len(self.eops[e]):
                    o = self.eops[e][ptr[e]]
                    if any(cnt.get(k, 0) < v for (k, v) in o.waits):
                        break
                    if o.dma_q is not None:
                        cnt[o.dma_key] = cnt.get(o.dma_key, 0) + 16
                    elif o.cc_sem is not None:
                        cnt[("cc", o.cc_sem)] = 1
                    elif o.need_inc:
                        cnt[("eng", e)] = cnt.get(("eng", e), 0) + 1
                    ptr[e] += 1
                    done += 1
                    progress = True
        if done != total:
            stuck = {e: (ptr[e], len(self.eops[e])) for e in ENGINES}
            detail = []
            for e in ENGINES:
                if ptr[e] < len(self.eops[e]):
                    o = self.eops[e][ptr[e]]
                    detail.append("%s@%d tag=%s waits=%s" % (e, ptr[e], o.tag, o.waits))
            raise RuntimeError("DEADLOCK: %s ; %s" % (stuck, " | ".join(detail)))

    def stats(self):
        s = {e: len(self.eops[e]) for e in ENGINES}
        s["drains"] = sum(1 for o in self.ops if o.drain)
        s["waits"] = sum(len(o.waits) for o in self.ops)
        s["incs"] = sum(1 for o in self.ops if o.need_inc and o.dma_q is None and o.cc_sem is None)
        s["dmas"] = sum(1 for o in self.ops if o.dma_q is not None)
        return s

    def emit(self, nc, block, sems):
        em = self

        def make(ename):
            def body(eng):
                for o in em.eops[ename]:
                    for (key, val) in o.waits:
                        eng.wait_ge(sems[key], val)
                    if o.drain and ename in DRAIN_ENGINES:
                        eng.drain()
                    inst = o.fn(eng)
                    if o.dma_q is not None:
                        inst.then_inc(sems[o.dma_key], 16)
                    elif o.cc_sem is not None:
                        inst.then_inc(sems[("cc", o.cc_sem)])
                    elif o.need_inc:
                        inst.then_inc(sems[("eng", ename)], 1)
            return body

        block.tensor(make("tensor"))
        block.vector(make("vector"))
        block.scalar(make("scalar"))
        block.gpsimd(make("gpsimd"))
        block.sync(make("sync"))


def host_constants():
    c = {}
    c["ident_bf"] = np.eye(128).astype(BFNP)
    gq = np.zeros((8, 128, CH), np.float64)
    gk = np.zeros((8, 128, CH), np.float64)
    ii = np.arange(CH, dtype=np.float64)
    for j in range(8):
        for half in range(2):
            h = 2 * j + half
            gq[j, 64 * half:64 * half + 64, :] = (GAMMAS[h] ** (ii + 1)) * SCALE
            gk[j, 64 * half:64 * half + 64, :] = GAMMAS[h] ** (-(ii + 1))
    c["Gq"] = gq.reshape(8 * 128, CH).astype(BFNP)
    c["Gk"] = gk.reshape(8 * 128, CH).astype(BFNP)
    c["caus"] = np.tril(np.ones((CH, CH), np.float32)).T.copy().astype(BFNP)
    gc = np.zeros((128, 512), np.float64)
    for h in range(H):
        gc[64 * (h % 2):64 * (h % 2) + 64, 64 * (h // 2):64 * (h // 2) + 64] = GAMMAS[h] ** CH
    c["Gc"] = gc.astype(BFNP)
    c["H2"] = np.stack([(np.arange(128) < 64), (np.arange(128) >= 64)]).astype(np.float32)
    c["H2T"] = np.ascontiguousarray(c["H2"].T)
    eoh = np.zeros((8, 8, 128), np.float32)
    for e in range(8):
        eoh[e, e, :] = 1.0
    c["Eoh"] = np.ascontiguousarray(eoh.transpose(1, 0, 2).reshape(8, 8 * 128)).astype(BFNP)
    return c


def core_constants(core):
    pos = core % 4
    wst = np.zeros((4, 128, 512), np.float64)
    for j in range(pos):
        p = pos - 1 - j
        for h in range(H):
            wst[j, 64 * (h % 2):64 * (h % 2) + 64, 64 * (h // 2):64 * (h // 2) + 64] = \
                GAMMAS[h] ** (2048 * p)
    return {"Wst": wst.reshape(4 * 128, 512).astype(BFNP)}


def pc_layout(v, n):
    return np.ascontiguousarray(np.asarray(v, np.float32).reshape(n // 128, 128).T)


POOL_N = 106400


def build_program(upto=None):
    nc = bass.Bass("TRN2", target_bir_lowering=False, debug=False)

    def din(name, shape, dt=F32):
        return nc.dram_tensor(name, shape, dt, kind="ExternalInput").ap()

    x_in = din("x_in", [TOK, D])
    import os as _osd
    KP1LOAD = bool(_osd.environ.get("KP1LOAD"))
    if KP1LOAD:
        xnfm_dbg = din("xnfm_dbg", [8 * 128, TOK], BF16)
    qwT = din("qwT", [D, D]); kwT = din("kwT", [D, D]); vwT = din("vwT", [D, D]); owT = din("owT", [D, D])
    w1T = din("w1T", [D, DFF]); w2T = din("w2T", [DFF, D])
    sW1_i = din("sW1", [D, EH]); sW2_i = din("sW2", [EH, D])
    rW1_i = din("rW1", [8, D, EH]); rW2_i = din("rW2", [8, EH, D])
    RWT_i = din("RWT", [D, 8])
    rb2_i = din("rb2", [8, D])
    ng_i = din("ng_pc", [128, 8]); nb_i = din("nb_pc", [128, 8])
    fg_i = din("fg_pc", [128, 8]); fb_i = din("fb_pc", [128, 8])
    ob_i = din("ob_pc", [128, 8]); b2_i = din("b2_pc", [128, 8]); sb2_i = din("sb2_pc", [128, 8])
    b1_i = din("b1_pc", [128, 16]); sb1_i = din("sb1_pc", [128, 4])
    rb1_i = din("rb1_pc", [8, 128, 4])
    eb_i = din("eb_pc", [8, 1])
    ident_i = din("ident_bf", [128, 128], BF16)
    Gq_i = din("Gq", [8 * 128, CH], BF16); Gk_i = din("Gk", [8 * 128, CH], BF16)
    caus_i = din("caus", [CH, CH], BF16)
    Gc_i = din("Gc", [128, 512], BF16); H2_i = din("H2", [2, 128]); H2T_i = din("H2T", [128, 2])
    Wst_i = din("Wst", [4 * 128, 512], BF16)
    Eoh_i = din("Eoh", [8, 8 * 128], BF16)

    out_d = nc.dram_tensor("out", [TOK, D], F32, kind="ExternalOutput").ap()

    wbf = {}
    for nm, shp in [("qwT", [D, D]), ("kwT", [D, D]), ("vwT", [D, D]), ("owT", [D, D]),
                    ("w1T", [D, DFF]), ("w2T", [DFF, D]), ("sW1", [D, EH]), ("sW2", [EH, D]),
                    ("RWT", [D, 8]), ("rb2", [8, D])]:
        wbf[nm] = nc.dram_tensor("bf_" + nm, shp, BF16).ap()
    for e in range(8):
        wbf["rW1_%d" % e] = nc.dram_tensor("bf_rW1_%d" % e, [D, EH], BF16).ap()
        wbf["rW2_%d" % e] = nc.dram_tensor("bf_rW2_%d" % e, [EH, D], BF16).ap()
    g_spill = nc.dram_tensor("g_spill", [8 * 128, TOK], BF16).ap()
    cc1_in = nc.dram_tensor("cc1_in", [2, 16], F32).ap()
    cc1_out = nc.dram_tensor("cc1_out", [2, 16], F32).ap()
    cc2_in = nc.dram_tensor("cc2_in", [2, 16], F32).ap()
    cc2_out = nc.dram_tensor("cc2_out", [2, 16], F32).ap()
    cc3_in = nc.dram_tensor("cc3_in", [128, 512], BF16).ap()
    cc3_out = nc.dram_tensor("cc3_out", [4 * 128, 512], BF16).ap()

    em = EM()
    groups4 = [[0, 1, 2, 3], [4, 5, 6, 7]]

    ctxs = []
    sb_cm = nc.sbuf_tensor("pool", [128, POOL_N], BF16)
    ctxs.append(sb_cm)
    sbuf = sb_cm.__enter__()
    pbs = []
    for i in range(8):
        cm = nc.psum_tensor("pb%d" % i, [128, 512], F32)
        ctxs.append(cm)
        pbs.append(cm.__enter__())

    def pb(i):
        return pbs[i][:, :]

    def pbbf(i):
        return pbs[i][:, :].bitcast(BF16)

    ptr = [0]

    def alloc(n):
        n = (n + 15) & ~15
        off = ptr[0]; ptr[0] += n
        assert ptr[0] <= POOL_N, "pool overflow %d" % ptr[0]
        return off

    A = alloc(16384)
    Bs = alloc(16384)
    Cs = alloc(16384)
    Ds = alloc(16384)
    Ws = alloc(16384)
    Us = alloc(13312)
    K_ident = alloc(128)
    K_onesb = alloc(128)
    K_onesf = alloc(256)
    K_Gq = alloc(8 * CH); K_Gk = alloc(8 * CH); K_caus = alloc(CH)
    K_Gc = alloc(512)
    K_H2 = alloc(256); K_H2T = alloc(16)
    K_eps = alloc(16)
    K_ng = alloc(16); K_nb = alloc(16); K_fg = alloc(16); K_fb = alloc(16)
    K_ob = alloc(16); K_b2 = alloc(16); K_sb2 = alloc(16)
    K_b1 = alloc(32); K_sb1 = alloc(8); K_rb1 = alloc(64)
    K_eb = alloc(16)
    K_rb2 = alloc(1024)
    K_rwt = alloc(64)
    K_sc1 = alloc(16); K_bs1 = alloc(16); K_sc2 = alloc(16); K_bs2 = alloc(16)
    K_cf = Bs + 13312
    K_identf = alloc(256)
    K_eoh = alloc(1024)
    V_s6 = alloc(64)
    V_mv = alloc(32)

    V_stat = alloc(64)
    V_gst = alloc(64)
    V_Srun = alloc(1024); V_Sbf = alloc(1024)
    V_Scc = alloc(1024); V_SccBf = alloc(1024)

    globals()["_OFF"] = {k: v for k, v in locals().items()
                        if isinstance(v, int) and (k.startswith('K_') or k.startswith('V_') or k in ('A', 'Bs', 'Cs', 'Ds', 'Ws', 'Us'))}

    def bfv(off, n):
        return sbuf[:, off:off + n]

    def fv(off, n):
        return sbuf[:, off:off + 2 * n].bitcast(F32)

    def ld(dst_ap, src_ap, res_w, res_r=(), q="qsync", eng="sync", tag="ld"):
        return em.op(eng, lambda eng, d=dst_ap, s=src_ap: eng.dma_start(out=d, in_=s),
                     reads=res_r, writes=(res_w,), dma_q=q, tag=tag)

    def st_dma(dst_ap, src_ap, res_w, res_r=(), q="qsync", eng="sync", tag="st"):
        return em.op(eng, lambda eng, d=dst_ap, s=src_ap: eng.dma_start(out=d, in_=s),
                     reads=res_r, writes=(res_w,), dma_q=q, tag=tag)

    def XN(j):
        return bfv(A + 2048 * j, 2048)

    def Yt(j):
        return bfv(Bs + 2048 * j, 2048)

    def Qh(j):
        return bfv(Cs + 2048 * j, 2048)

    conv = [("RWT", RWT_i), ("sW1", sW1_i), ("sW2", sW2_i), ("rb2", rb2_i)]
    for e in range(8):
        conv += [("rW1_%d" % e, rW1_i[e]), ("rW2_%d" % e, rW2_i[e])]
    conv += [("qwT", qwT), ("kwT", kwT), ("vwT", vwT), ("owT", owT), ("w1T", w1T), ("w2T", w2T)]
    import os as _os0
    if _os0.environ.get("KCONVMIN"):
        conv = [c for c in conv if c[0] in ("RWT", "sW1", "sW2", "rb2")]
    for nm, src in conv:
        em.op("gpsimd", lambda eng, d=wbf[nm], s=src: eng.dma_start(out=d[:, :], in_=s[:, :]),
              writes=("hbm_" + nm,), dma_q="qgp", tag="conv_" + nm)

    ld(bfv(K_ident, 128), ident_i[:, :], "ident")
    ld(bfv(K_Gq, 8 * CH).rearrange("p (j c) -> p j c", j=8),
       Gq_i.rearrange("(j p) c -> p j c", p=128), "Gq")
    ld(bfv(K_Gk, 8 * CH).rearrange("p (j c) -> p j c", j=8),
       Gk_i.rearrange("(j p) c -> p j c", p=128), "Gk")
    ld(bfv(K_caus, CH), caus_i[:, :], "caus")
    ld(bfv(K_Gc, 512), Gc_i[:, :], "Gc")
    ld(fv(K_H2, 128)[0:2, :], H2_i[:, :], "H2")
    ld(fv(K_H2T, 8)[:, 0:2], H2T_i[:, :], "H2T")
    ld(fv(K_ng, 8), ng_i[:, :], "ng"); ld(fv(K_nb, 8), nb_i[:, :], "nb")
    ld(fv(K_fg, 8), fg_i[:, :], "fg"); ld(fv(K_fb, 8), fb_i[:, :], "fb")
    ld(fv(K_ob, 8), ob_i[:, :], "obp"); ld(fv(K_b2, 8), b2_i[:, :], "b2p"); ld(fv(K_sb2, 8), sb2_i[:, :], "sb2p")
    ld(fv(K_b1, 16), b1_i[:, :], "b1p"); ld(fv(K_sb1, 4), sb1_i[:, :], "sb1p")
    for e in range(8):
        ld(fv(K_rb1 + 8 * e, 4), rb1_i[e], "rb1p%d" % e)
    ld(fv(K_eb, 8)[0:8, 0:1], eb_i[:, :], "ebp")
    ld(bfv(K_rb2, 1024)[0:8, :], wbf["rb2"][:, :], "rb2p", res_r=("hbm_rb2",))
    ld(bfv(K_rwt, 64).rearrange("p (j c) -> p j c", j=8),
       wbf["RWT"].rearrange("(j p) c -> p j c", p=128), "rwt", res_r=("hbm_RWT",))
    ld(bfv(K_eoh, 1024)[0:8, :], Eoh_i[:, :], "eoh")
    em.op("vector", lambda eng: eng.tensor_copy(fv(K_identf, 128), bfv(K_ident, 128)),
          reads=("ident",), writes=("identf",), tag="idf")
    em.op("vector", lambda eng: eng.memset(bfv(K_onesb, 128), 1.0), writes=("onesb",))
    em.op("vector", lambda eng: eng.memset(fv(K_onesf, 128), 1.0), writes=("onesf",))
    em.op("vector", lambda eng: eng.memset(fv(K_eps, 8), EPS), writes=("eps",))
    em.op("vector", lambda eng: eng.memset(fv(V_Srun, 512), 0.0), writes=("Srun",))
    em.op("vector", lambda eng: eng.memset(bfv(V_Sbf, 512), 0.0), writes=("Sbf",))

    em.barrier()
    em.mark("P0")

    if KP1LOAD:
        ld(bfv(A, 16384).rearrange("p (j t) -> p j t", j=8),
           xnfm_dbg.rearrange("(j p) t -> p j t", p=128), "XNL", tag="ldxn")
        for i in range(16):
            em.op("vector", lambda eng: eng.nop(), reads=("XNL",), writes=("XN/%d" % i,), tag="xnal")
        P1RANGE = range(0)
    else:
        P1RANGE = range(NTC)
    for tc in P1RANGE:
        buf = tc % 2
        xt = fv(Bs + buf * 8192, 4096)
        rx = "xtok%d" % buf
        ld(xt.rearrange("p (c d) -> p c d", c=4),
           x_in[tc * TC:(tc + 1) * TC, :].rearrange("(c p) d -> p c d", p=128), rx, tag="ldx%d" % tc)
        for c in range(4):
            xsub = xt[:, c * 1024:(c + 1) * 1024]
            s6 = fv(V_s6, 32)
            mv = fv(V_mv, 16)
            em.op("vector", lambda eng, x_=xsub, s_=s6: eng.bn_stats(s_[:, 0:6], x_[:, 0:512]),
                  reads=(rx,), writes=("s6a",), tag="bn")
            em.op("vector", lambda eng, x_=xsub, s_=s6: eng.bn_stats(s_[:, 6:12], x_[:, 512:1024]),
                  reads=(rx,), writes=("s6b",), tag="bn")
            em.op("vector", lambda eng, s_=s6, m_=mv: eng.bn_aggr(m_[:, 0:2], s_[:, 0:12]),
                  reads=("s6a", "s6b"), writes=("mv",), tag="bnag")
            em.op("scalar", lambda eng, m_=mv: eng.activation(out=m_[:, 2:3], in_=m_[:, 1:2],
                                                              func=AF.Sqrt, bias=fv(K_eps, 1)),
                  reads=("mv", "eps"), writes=("mvsd",), tag="sq")
            em.op("vector", lambda eng, m_=mv: eng.reciprocal(m_[:, 3:4], m_[:, 2:3]),
                  reads=("mvsd",), writes=("mvrs",), tag="rc")
            xnt = bfv(Us + c * 1024, 1024)
            em.op("vector", lambda eng, x_=xsub, m_=mv, o_=xnt:
                  eng.tensor_scalar(out=o_, in0=x_, scalar1=m_[:, 0:1], scalar2=m_[:, 3:4],
                                    op0=ALU.subtract, op1=ALU.mult),
                  reads=(rx, "mv", "mvrs"), writes=("xnt%d" % c,), tag="lnap")
            bank = c % 2
            for j in range(8):
                em.op("tensor", lambda eng, x_=xnt, j_=j, b_=bank:
                      eng.transpose(pbbf(b_)[:, 128 * j_:128 * (j_ + 1)],
                                    x_[:, 128 * j_:128 * (j_ + 1)], bfv(K_ident, 128)),
                      reads=("xnt%d" % c, "ident"), writes=("pb%d" % bank,), tag="tpx")
            off = tc * TC + c * 128
            em.op("vector", lambda eng, b_=bank, o_=off:
                  eng.tensor_copy(bfv(A, 16384).rearrange("p (j t) -> p j t", j=8)[:, :, o_:o_ + 128],
                                  pbbf(b_).rearrange("p (j t) -> p j t", j=8)),
                  reads=("pb%d" % bank,), writes=("XN/%d" % (tc * 4 + c),), tag="evx")

    XN_ALL = tuple("XN/%d" % i for i in range(16))

    def gn_stats_pre(tag, tile_fn, src_res, cc_in_d, cc_out_d):
        chm = fv(V_stat, 32)
        s6 = fv(V_s6, 32)
        mv = fv(V_mv, 16)
        for j in range(8):
            for q in range(4):
                em.op("vector", lambda eng, j_=j, q_=q, s_=s6:
                      eng.bn_stats(s_[:, 6 * q_:6 * q_ + 6], tile_fn(j_)[:, 512 * q_:512 * (q_ + 1)]),
                      reads=src_res, writes=(tag + "s6%d" % q,), tag=tag + "bn")
            em.op("vector", lambda eng, s_=s6, m_=mv: eng.bn_aggr(m_[:, 0:2], s_[:, 0:24]),
                  reads=tuple(tag + "s6%d" % q for q in range(4)), writes=(tag + "mv",), tag=tag + "ag")
            em.op("vector", lambda eng, j_=j, c_=chm, m_=mv: eng.tensor_copy(c_[:, j_:j_ + 1], m_[:, 0:1]),
                  reads=(tag + "mv",), writes=(tag + "cm%d" % j,), tag=tag + "cm")
            em.op("vector", lambda eng, j_=j, c_=chm, m_=mv:
                  eng.scalar_tensor_tensor(out=c_[:, 8 + j_:9 + j_], in0=m_[:, 0:1],
                                           scalar=m_[:, 0:1], in1=m_[:, 1:2],
                                           op0=ALU.mult, op1=ALU.add),
                  reads=(tag + "mv",), writes=(tag + "ce%d" % j,), tag=tag + "ce")
        chres = tuple(tag + "cm%d" % j for j in range(8)) + tuple(tag + "ce%d" % j for j in range(8))
        em.op("tensor", lambda eng, c_=chm: eng.matmul(pb(6)[0:2, 0:16], fv(K_H2T, 8)[:, 0:2],
                                                       c_[:, 0:16], start=True, stop=True),
              reads=chres + ("H2T",), writes=("pb6",), tag=tag + "red")
        gg = fv(V_gst, 32)
        em.op("vector", lambda eng, g_=gg: eng.tensor_scalar(out=g_[0:2, 0:16], in0=pb(6)[0:2, 0:16],
                                                             scalar1=1.0 / 64.0, scalar2=None, op0=ALU.mult),
              reads=("pb6",), writes=(tag + "pk",), tag=tag + "pk")
        st_dma(cc_in_d[:, :], gg[0:2, 0:16], tag + "cw", res_r=(tag + "pk",), tag=tag + "cw")
        em.op("gpsimd", lambda eng: eng.collective_compute(
              "AllReduce", ALU.add, replica_groups=groups4,
              ins=[cc_in_d.opt()], outs=[cc_out_d.opt()]),
              reads=(tag + "cw",), writes=(tag + "AR",), cc=True, tag=tag + "AR")
        ld(gg[0:2, 16:32], cc_out_d[:, :], tag + "cr", res_r=(tag + "AR",), tag=tag + "cr")

    def gn_stats_fin(tag, gamres, betres, g_off, b_off, sc_off, bs_off):
        gg = fv(V_gst, 32)
        em.op("vector", lambda eng, g_=gg: eng.tensor_scalar(out=g_[0:2, 16:32], in0=g_[0:2, 16:32],
                                                             scalar1=0.25, scalar2=None, op0=ALU.mult),
              reads=(tag + "cr",), writes=(tag + "mu",), tag=tag + "mu")
        em.op("vector", lambda eng, g_=gg: eng.tensor_tensor(out=g_[0:2, 8:16], in0=g_[0:2, 16:24],
                                                             in1=g_[0:2, 16:24], op=ALU.mult),
              reads=(tag + "mu",), writes=(tag + "m2",), tag=tag + "m2")
        em.op("vector", lambda eng, g_=gg: eng.tensor_tensor(out=g_[0:2, 8:16], in0=g_[0:2, 24:32],
                                                             in1=g_[0:2, 8:16], op=ALU.subtract),
              reads=(tag + "m2", tag + "mu"), writes=(tag + "va",), tag=tag + "va")
        em.op("scalar", lambda eng, g_=gg: eng.activation(out=g_[0:2, 8:16], in_=g_[0:2, 8:16],
                                                          func=AF.Sqrt, bias=fv(K_eps, 1)[0:2, :]),
              reads=(tag + "va", "eps"), writes=(tag + "sd",), tag=tag + "sd")
        em.op("vector", lambda eng, g_=gg: eng.reciprocal(g_[0:2, 8:16], g_[0:2, 8:16]),
              reads=(tag + "sd",), writes=(tag + "rs",), tag=tag + "rs")
        em.op("tensor", lambda eng, g_=gg: eng.matmul(pb(6)[:, 0:8], fv(K_H2, 128)[0:2, :],
                                                      g_[0:2, 8:16], start=True, stop=True),
              reads=(tag + "rs", "H2"), writes=("pb6",), tag=tag + "xr")
        em.op("tensor", lambda eng, g_=gg: eng.matmul(pb(7)[:, 0:8], fv(K_H2, 128)[0:2, :],
                                                      g_[0:2, 16:24], start=True, stop=True),
              reads=(tag + "mu", "H2"), writes=("pb7",), tag=tag + "xm")
        em.op("vector", lambda eng: eng.tensor_tensor(out=fv(sc_off, 8), in0=pb(6)[:, 0:8],
                                                      in1=fv(g_off, 8), op=ALU.mult),
              reads=("pb6", gamres), writes=(tag + "sc",), tag=tag + "sc")
        em.op("vector", lambda eng: eng.tensor_tensor(out=fv(V_mv, 16)[:, 0:8], in0=pb(7)[:, 0:8],
                                                      in1=fv(sc_off, 8), op=ALU.mult),
              reads=("pb7", tag + "sc", tag + "mv"), writes=(tag + "ms",), tag=tag + "ms")
        em.op("vector", lambda eng: eng.tensor_tensor(out=fv(bs_off, 8), in0=fv(b_off, 8),
                                                      in1=fv(V_mv, 16)[:, 0:8], op=ALU.subtract),
              reads=(tag + "ms", betres), writes=(tag + "bs",), tag=tag + "bs")

    em.mark("P1")
    em.barrier()
    em.mark("P1b")

    def lrow(tc):
        return fv(Us + tc * 1024, 512)

    import os as _os2
    P2RANGE = range(0) if _os2.environ.get("KNOP2") else range(NTC)
    for tc in P2RANGE:
        for k in range(8):
            em.op("tensor", lambda eng, k_=k, t_=tc:
                  eng.matmul(pb(0)[0:8, :], bfv(K_rwt, 64)[:, 8 * k_:8 * k_ + 8],
                             XN(k_)[:, t_ * TC:(t_ + 1) * TC], start=(k_ == 0), stop=(k_ == 7)),
                  reads=("rwt", "XN/%d" % (tc * 4),) + XN_ALL, writes=("pb0",), tag="lg")
        em.op("vector", lambda eng, t_=tc: eng.tensor_scalar(out=lrow(t_)[0:8, :], in0=pb(0)[0:8, :],
                                                             scalar1=fv(K_eb, 8)[0:8, 0:1], scalar2=None,
                                                             op0=ALU.add),
              reads=("pb0", "ebp"), writes=("lr%d" % tc,), tag="lgev")
    em.mark("P2a")
    for tc in P2RANGE:
        lr = lrow(tc)
        ab = fv(Us + 4096 + 1024, 512)
        em.op("scalar", lambda eng, l_=lr, a_=ab: eng.activation(out=a_[0:8, :], in_=l_[0:8, :], func=AF.Abs),
              reads=("lr%d" % tc,), writes=("abR",), tag="abs")
        em.op("scalar", lambda eng, a_=ab: eng.activation(out=a_[0:8, :], in_=a_[0:8, :], func=AF.Exp,
                                                          scale=-1.0),
              reads=("abR",), writes=("abR",), tag="exp")
        em.op("scalar", lambda eng, a_=ab: eng.activation(out=a_[0:8, :], in_=a_[0:8, :], func=AF.Ln,
                                                          bias=fv(K_onesf, 1)[0:8, :]),
              reads=("abR", "onesf"), writes=("abR",), tag="ln")
        em.op("vector", lambda eng, l_=lr: eng.tensor_scalar(out=l_[0:8, :], in0=l_[0:8, :],
                                                             scalar1=0.0, scalar2=None, op0=ALU.max),
              reads=("lr%d" % tc,), writes=("lr%d" % tc,), tag="relu")
        em.op("vector", lambda eng, l_=lr, a_=ab: eng.tensor_tensor(out=l_[0:8, :], in0=l_[0:8, :],
                                                                    in1=a_[0:8, :], op=ALU.add),
              reads=("lr%d" % tc, "abR"), writes=("lr%d" % tc,), tag="sp")
    for tc in P2RANGE:
        lr = lrow(tc)
        em.op("scalar", lambda eng, l_=lr: eng.activation(out=l_[0:8, :], in_=l_[0:8, :], func=AF.Sqrt),
              reads=("lr%d" % tc,), writes=("lr%d" % tc, "ac%d" % tc), tag="rsq")
    em.mark("P2sp")
    atok = fv(Us + 4096 + 2048, 1024)
    for tc in P2RANGE:
        lr = lrow(tc)
        for s in range(4):
            em.op("tensor", lambda eng, l_=lr, s_=s:
                  eng.transpose(pb(1)[:, 8 * s_:8 * s_ + 8], l_[0:8, 128 * s_:128 * (s_ + 1)],
                                fv(K_identf, 128)[0:8, 0:8]),
                  reads=("ac%d" % tc, "identf"), writes=("pb1",), tag="tpl")
        em.op("vector", lambda eng: eng.tensor_copy(atok[:, 0:32], pb(1)[:, 0:32]),
              reads=("pb1",), writes=("atokR",), tag="evl")
        for s in range(4):
            a8 = atok[:, 8 * s:8 * s + 8]
            m1 = atok[:, 32 + s:33 + s]
            x2 = atok[:, 40 + 8 * s:48 + 8 * s]
            em.op("vector", lambda eng, a_=a8, m_=m1: eng.reduce_max(m_, a_, axis=AX.X),
                  reads=("atokR",), writes=("atokR",), tag="m1")
            em.op("vector", lambda eng, a_=a8, m_=m1, x_=x2:
                  eng.tensor_scalar(out=x_, in0=a_, scalar1=m_, scalar2=None, op0=ALU.is_ge),
                  reads=("atokR",), writes=("atokR",), tag="ge")
            em.op("vector", lambda eng, a_=a8, x_=x2:
                  eng.scalar_tensor_tensor(out=x_, in0=x_, scalar=-1e30, in1=a_, op0=ALU.mult, op1=ALU.add),
                  reads=("atokR",), writes=("atokR",), tag="x2")
            em.op("vector", lambda eng, m_=m1, x_=x2: eng.reduce_max(m_, x_, axis=AX.X),
                  reads=("atokR",), writes=("atokR",), tag="m2")
            em.op("vector", lambda eng, a_=a8, m_=m1, x_=x2:
                  eng.tensor_scalar(out=x_, in0=a_, scalar1=m_, scalar2=None, op0=ALU.is_ge),
                  reads=("atokR",), writes=("atokR",), tag="kp")
            em.op("vector", lambda eng, a_=a8, x_=x2:
                  eng.tensor_tensor(out=x_, in0=x_, in1=a_, op=ALU.mult),
                  reads=("atokR",), writes=("atokR",), tag="cf")
        em.mark("P2t%d" % tc)
        for s in range(4):
            em.op("tensor", lambda eng, s_=s:
                  eng.transpose(pb(2)[0:8, 128 * s_:128 * (s_ + 1)], atok[:, 40 + 8 * s_:48 + 8 * s_],
                                fv(K_identf, 128)),
                  reads=("atokR", "identf"), writes=("pb2",), tag="tpc")
        em.op("vector", lambda eng, t_=tc:
              eng.tensor_copy(bfv(K_cf, 2048)[0:8, t_ * TC:(t_ + 1) * TC], pb(2)[0:8, 0:512]),
              reads=("pb2",), writes=("cfT%d" % tc,), tag="evc")
    if _os2.environ.get("KNOP2"):
        em.op("vector", lambda eng: eng.memset(bfv(K_cf, 2048), 0.0),
              writes=tuple("cfT%d" % t for t in range(NTC)), tag="cfz")
    CF_ALL = tuple("cfT%d" % t for t in range(NTC))

    em.mark("P2")
    import os as _os
    if not _os.environ.get("KNOGN"):
        gn_stats_pre("g1", XN, XN_ALL, cc1_in, cc1_out)

    em.barrier()
    em.mark("P2b")

    ld(bfv(Us, 4096).rearrange("p (k c) -> p k c", k=8),
       wbf["sW1"].rearrange("(k p) c -> p k c", p=128), "sW1s", res_r=("hbm_sW1",))
    ld(bfv(Us + 4096, 4096).rearrange("p (k c) -> p k c", k=4),
       wbf["sW2"].rearrange("(k p) c -> p k c", p=128), "sW2s", res_r=("hbm_sW2",))

    def exp_w1(slot):
        return bfv(Ws + 8192 * slot, 4096)

    def exp_w2(slot):
        return bfv(Ws + 8192 * slot + 4096, 4096)

    for e in range(2):
        s = e % 2
        ld(exp_w1(s).rearrange("p (k c) -> p k c", k=8),
           wbf["rW1_%d" % e].rearrange("(k p) c -> p k c", p=128), "Wa%d" % s, res_r=("hbm_rW1_%d" % e,), tag="lde%d" % e)

    def hs_t(k):
        return bfv(Bs + 1024 * k, 1024)

    def ht_t(e, k):
        base = Cs if e < 4 else Ds
        return bfv(base + (e % 4) * 4096 + 1024 * k, 1024)

    em.mark("P3w")
    for mt in range(NMTC):
        tslc = slice(mt * MTC, (mt + 1) * MTC)
        for hc in range(4):
            for th in range(2):
                bank = (hc * 2 + th) % 2
                for k in range(8):
                    em.op("tensor", lambda eng, k_=k, h_=hc, t_=th, b_=bank, m_=mt:
                          eng.matmul(pb(b_), bfv(Us, 4096)[:, 512 * k_ + 128 * h_:512 * k_ + 128 * h_ + 128],
                                     XN(k_)[:, m_ * MTC + 512 * t_: m_ * MTC + 512 * t_ + 512],
                                     start=(k_ == 0), stop=(k_ == 7)),
                          reads=("sW1s",) + XN_ALL, writes=("pb%d" % bank,), tag="sh1")
                em.op("scalar", lambda eng, h_=hc, t_=th, b_=bank, m_=mt:
                      eng.activation(out=hs_t(h_)[:, 512 * t_:512 * t_ + 512], in_=pb(b_),
                                     func=AF.Silu, bias=fv(K_sb1, 4)[:, h_:h_ + 1]),
                      reads=("pb%d" % bank, "sb1p"), writes=("hs/%d/%d" % (hc, th),), tag="sh1ev")
                if _os2.environ.get("KHSCOPY"):
                    em.op("vector", lambda eng, h_=hc, t_=th, b_=bank:
                          eng.tensor_copy(bfv(Cs + 1024 * h_, 1024)[:, 512 * t_:512 * t_ + 512],
                                          pb(b_)),
                          reads=("pb%d" % bank,), writes=("hscp/%d/%d" % (hc, th),), tag="hscp")
        em.mark("P3s%d" % mt)
        for e in range(8):
            s = e % 2
            for th in range(2):
                em.op("tensor", lambda eng, e_=e, t_=th, m_=mt:
                      eng.matmul(pb(2 + t_), bfv(K_eoh, 1024)[0:8, 128 * e_:128 * e_ + 128],
                                 bfv(K_cf, 2048)[0:8, m_ * MTC + 512 * t_:m_ * MTC + 512 * t_ + 512],
                                 start=True, stop=True),
                      reads=("eoh",) + CF_ALL, writes=("pb%d" % (2 + th),), tag="cfb")
            em.op("vector", lambda eng: eng.tensor_copy(bfv(Bs + 12288, 1024)[:, 0:512], pb(2)),
                  reads=("pb2",), writes=("cfb0",), tag="cfbe")
            em.op("vector", lambda eng: eng.tensor_copy(bfv(Bs + 12288, 1024)[:, 512:1024], pb(3)),
                  reads=("pb3",), writes=("cfb1",), tag="cfbe")
            for hc in range(4):
                for th in range(2):
                    bank = (hc * 2 + th) % 2
                    for k in range(8):
                        em.op("tensor", lambda eng, k_=k, h_=hc, t_=th, b_=bank, m_=mt, s_=s:
                              eng.matmul(pb(b_), exp_w1(s_)[:, 512 * k_ + 128 * h_:512 * k_ + 128 * h_ + 128],
                                         XN(k_)[:, m_ * MTC + 512 * t_: m_ * MTC + 512 * t_ + 512],
                                         start=(k_ == 0), stop=(k_ == 7)),
                              reads=("Wa%d" % s,) + XN_ALL, writes=("pb%d" % bank,), tag="ex1")
                    em.op("scalar", lambda eng, h_=hc, t_=th, b_=bank, e_=e:
                          eng.activation(out=bfv(Us + 8192, 1024)[:, 512 * t_:512 * t_ + 512], in_=pb(b_),
                                         func=AF.Silu, bias=fv(K_rb1 + 8 * e_, 4)[:, h_:h_ + 1]),
                          reads=("pb%d" % bank, "rb1p%d" % e), writes=("hsil%d" % th,), tag="ex1ev")
                    em.op("vector", lambda eng, h_=hc, t_=th, e_=e:
                          eng.tensor_tensor(out=ht_t(e_, h_)[:, 512 * t_:512 * t_ + 512],
                                            in0=bfv(Us + 8192, 1024)[:, 512 * t_:512 * t_ + 512],
                                            in1=bfv(Bs + 12288, 1024)[:, 512 * t_:512 * t_ + 512],
                                            op=ALU.mult),
                          reads=("hsil%d" % th, "cfb%d" % th), writes=("ht/%d/%d/%d" % (e, hc, th),), tag="excf")
            em.mark("P3x%d_%d" % (mt, e))
            en = e + 2 if e + 2 <= 7 else (e + 2 - 8 if mt == 0 else None)
            if en is not None:
                ld(exp_w1(s).rearrange("p (k c) -> p k c", k=8),
                   wbf["rW1_%d" % en].rearrange("(k p) c -> p k c", p=128), "Wa%d" % s,
                   res_r=("hbm_rW1_%d" % en,), tag="lde%d" % en)
        em.mark("P3e%d" % mt)
        HT_ALL = tuple("ht/%d/%d/%d" % (e, hc, th) for e in range(8) for hc in range(4) for th in range(2))
        HS_ALL = tuple("hs/%d/%d" % (hc, th) for hc in range(4) for th in range(2))
        for th in range(2):
            if th == 0 and mt == 0:
                for e in range(2):
                    ld(exp_w2(e).rearrange("p (k c) -> p k c", k=4),
                       wbf["rW2_%d" % e].rearrange("(k p) c -> p k c", p=128), "Wb%d" % e,
                       res_r=("hbm_rW2_%d" % e,), tag="ldw2_%d" % e)
            for hk in range(4):
                for dc in range(8):
                    em.op("tensor", lambda eng, k_=hk, d_=dc, t_=th:
                          eng.matmul(pb(d_), bfv(Us + 4096, 4096)[:, 1024 * k_ + 128 * d_:1024 * k_ + 128 * d_ + 128],
                                     hs_t(k_)[:, 512 * t_:512 * t_ + 512], start=(k_ == 0), stop=False),
                          reads=("sW2s",) + HS_ALL, writes=("pb%d" % dc,), tag="sh2")
            for e in range(8):
                s = e % 2
                for hk in range(4):
                    for dc in range(8):
                        em.op("tensor", lambda eng, e_=e, k_=hk, d_=dc, t_=th, s_=s:
                              eng.matmul(pb(d_), exp_w2(s_)[:, 1024 * k_ + 128 * d_:1024 * k_ + 128 * d_ + 128],
                                         ht_t(e_, k_)[:, 512 * t_:512 * t_ + 512], start=False, stop=False),
                              reads=("Wb%d" % s,) + HT_ALL, writes=("pb%d" % dc,), tag="ex2")
                nxt = None
                if e + 2 <= 7:
                    nxt = e + 2
                elif not (mt == 1 and th == 1):
                    nxt = e + 2 - 8
                if nxt is not None:
                    ld(exp_w2(s).rearrange("p (k c) -> p k c", k=4),
                       wbf["rW2_%d" % nxt].rearrange("(k p) c -> p k c", p=128), "Wb%d" % s,
                       res_r=("hbm_rW2_%d" % nxt,), tag="ldw2n%d" % nxt)
            for dc in range(8):
                em.op("tensor", lambda eng, d_=dc, t_=th, m_=mt:
                      eng.matmul(pb(d_), bfv(K_rb2, 1024)[0:8, 128 * d_:128 * d_ + 128],
                                 bfv(K_cf, 2048)[0:8, m_ * MTC + 512 * t_:m_ * MTC + 512 * t_ + 512],
                                 start=False, stop=True),
                      reads=("rb2p",) + CF_ALL, writes=("pb%d" % dc,), tag="rb2mm")
                em.op("scalar", lambda eng, d_=dc, t_=th:
                      eng.activation(out=bfv(Bs + 4096 + 1024 * d_, 1024)[:, 512 * t_:512 * t_ + 512],
                                     in_=pb(d_), func=AF.Silu, bias=fv(K_sb2, 8)[:, d_:d_ + 1]),
                      reads=("pb%d" % dc, "sb2p"), writes=("gst/%d/%d" % (dc, th),), tag="gev")
        st_dma(g_spill.rearrange("(j p) t -> p j t", p=128)[:, :, mt * MTC:(mt + 1) * MTC],
               bfv(Bs + 4096, 8192).rearrange("p (j t) -> p j t", j=8),
               "gsp%d" % mt, res_r=tuple("gst/%d/%d" % (dc, th) for dc in range(8) for th in range(2)),
               tag="gspill%d" % mt)

    em.mark("P3")
    gn_stats_fin("g1", "ng", "nb", K_ng, K_nb, K_sc1, K_bs1)

    em.barrier()
    em.mark("P3b")

    ld(bfv(Ws, 8192).rearrange("p (k c) -> p k c", k=8),
       wbf["qwT"].rearrange("(k p) c -> p k c", p=128), "Wq", res_r=("hbm_qwT",))
    ld(bfv(Ws + 8192, 8192).rearrange("p (k c) -> p k c", k=8),
       wbf["kwT"].rearrange("(k p) c -> p k c", p=128), "Wk", res_r=("hbm_kwT",))
    ld(bfv(Us, 8192).rearrange("p (k c) -> p k c", k=8),
       wbf["vwT"].rearrange("(k p) c -> p k c", p=128), "Wv", res_r=("hbm_vwT",))

    def a1t(k):
        return bfv(Us + 8192 + 512 * k, 512)

    def khat_e(j):
        return bfv(Ds + 512 * j, 512)

    def khat_o(j):
        return bfv(Us + 8192 + 512 * j, 512)

    def vtok(sub):
        return bfv(Ds + 4096 + 1024 * sub, 1024)

    def ktok(sub):
        return bfv(Ds + 8192 + 1024 * sub, 1024)

    def ks(dc):
        return bfv(Ds + 12288 + 512 * dc, 512)

    Asb = bfv(Ds + 12288, 2048)
    scr1 = bfv(Us + 12288, 512)
    scr2 = bfv(Us + 12288 + 512, 512)

    Srun = fv(V_Srun, 512)
    Sbf = bfv(V_Sbf, 1024)
    em.op("vector", lambda eng: eng.memset(Sbf, 0.0), writes=("SbfR",), tag="Sbfz")

    YBANKS = (0, 1, 6, 7)

    def yslot(h):
        return YBANKS[2 * (h % 2) + (h // 8)], 128 * ((h // 2) % 4)

    for tc in range(NTC):
        for k in range(8):
            em.op("vector", lambda eng, k_=k, t_=tc:
                  eng.tensor_scalar(out=a1t(k_), in0=XN(k_)[:, t_ * TC:(t_ + 1) * TC],
                                    scalar1=fv(K_sc1, 8)[:, k_:k_ + 1], scalar2=fv(K_bs1, 8)[:, k_:k_ + 1],
                                    op0=ALU.mult, op1=ALU.add),
                  reads=("XN/%d" % (tc * 4), "XN/%d" % (tc * 4 + 1), "XN/%d" % (tc * 4 + 2),
                         "XN/%d" % (tc * 4 + 3), "g1sc", "g1bs"), writes=("a1/%d" % k,), tag="a1")
        A1_ALL = tuple("a1/%d" % k for k in range(8))
        for sub in range(4):
            for th in range(2):
                bank = (sub * 2 + th) % 2
                for k in range(8):
                    em.op("tensor", lambda eng, k_=k, s_=sub, t_=th, b_=bank:
                          eng.matmul(pb(b_), a1t(k_)[:, 128 * s_:128 * s_ + 128],
                                     bfv(Us, 8192)[:, 1024 * k_ + 512 * t_:1024 * k_ + 512 * t_ + 512],
                                     start=(k_ == 0), stop=(k_ == 7)),
                          reads=A1_ALL + ("Wv",), writes=("pb%d" % bank,), tag="vmm")
                em.op("vector", lambda eng, s_=sub, t_=th, b_=bank:
                      eng.tensor_copy(vtok(s_)[:, 512 * t_:512 * t_ + 512], pb(b_)),
                      reads=("pb%d" % bank,), writes=("v/%d" % sub,), tag="vev")
        for which in ("q", "k"):
            woff = Ws + (8192 if which == "k" else 0)
            wres = "Wk" if which == "k" else "Wq"
            gko = K_Gk if which == "k" else K_Gq
            for dc in range(8):
                bank = dc % 2
                for k in range(8):
                    em.op("tensor", lambda eng, k_=k, d_=dc, b_=bank, w_=woff, t_=tc:
                          eng.matmul(pb(b_), bfv(w_, 8192)[:, 1024 * k_ + 128 * d_:1024 * k_ + 128 * d_ + 128],
                                     a1t(k_), start=(k_ == 0), stop=(k_ == 7)),
                          reads=A1_ALL + (wres,), writes=("pb%d" % bank,), tag=which + "mm")
                em.op("vector", lambda eng, d_=dc, b_=bank: eng.tensor_copy(ks(d_), pb(b_)),
                      reads=("pb%d" % bank,), writes=("ks%d" % dc,), tag=which + "ev")
                em.op("scalar", lambda eng, d_=dc: eng.activation(out=scr1, in_=ks(d_), func=AF.Square),
                      reads=("ks%d" % dc,), writes=("scr1",), tag=which + "sq")
                em.op("tensor", lambda eng, d_=dc: eng.matmul(pb(2)[0:1, :], bfv(K_onesb, 128)[:, 0:1],
                                                              scr1, start=(d_ == 0), stop=(d_ == 7)),
                      reads=("scr1", "onesb"), writes=("pb2",), tag=which + "ss")
            rrow = fv(Ds + 8192, 1024)
            em.op("scalar", lambda eng: eng.activation(out=rrow[0:1, 0:512], in_=pb(2)[0:1, :], func=AF.Sqrt),
                  reads=("pb2",), writes=("rnrm",), tag=which + "nsq")
            em.op("vector", lambda eng: eng.reciprocal(rrow[0:1, 512:1024], rrow[0:1, 0:512]),
                  reads=("rnrm",), writes=("rrec",), tag=which + "nrc")
            em.op("tensor", lambda eng: eng.matmul(pb(3), fv(K_onesf, 128)[0:1, :], rrow[0:1, 512:1024],
                                                   start=True, stop=True),
                  reads=("rrec", "onesf"), writes=("pb3",), tag=which + "bc")
            em.op("vector", lambda eng: eng.tensor_copy(scr2, pb(3)),
                  reads=("pb3",), writes=("rsb",), tag=which + "bce")
            for dc in range(8):
                em.op("vector", lambda eng, d_=dc, g_=gko:
                      eng.tensor_tensor(out=scr1.rearrange("p (r c) -> p r c", r=4),
                                        in0=scr2.rearrange("p (r c) -> p r c", r=4),
                                        in1=bfv(g_, 8 * CH)[:, None, 128 * d_:128 * d_ + 128].broadcast_to((128, 4, 128)),
                                        op=ALU.mult),
                      reads=("rsb", "Gq" if which == "q" else "Gk"), writes=("scr1",), tag=which + "fk")
                if which == "q":
                    em.op("vector", lambda eng, d_=dc, t_=tc:
                          eng.tensor_tensor(out=Qh(d_)[:, t_ * TC:(t_ + 1) * TC], in0=ks(d_), in1=scr1,
                                            op=ALU.mult),
                          reads=("scr1", "ks%d" % dc), writes=("qh/%d/%d" % (tc, dc),), tag="qhat")
                else:
                    em.op("vector", lambda eng, d_=dc:
                          eng.tensor_tensor(out=khat_e(d_)[0:64, :], in0=ks(d_)[0:64, :],
                                            in1=scr1[0:64, :], op=ALU.mult),
                          reads=("scr1", "ks%d" % dc), writes=("khe%d" % dc,), tag="khe")
                    em.op("vector", lambda eng, d_=dc:
                          eng.tensor_tensor(out=khat_o(d_)[64:128, :], in0=ks(d_)[64:128, :],
                                            in1=scr1[64:128, :], op=ALU.mult),
                          reads=("scr1", "ks%d" % dc, "a1/%d" % dc), writes=("kho%d" % dc,), tag="kho")
        for dc in range(8):
            em.op("vector", lambda eng, d_=dc: eng.memset(khat_o(d_)[0:64, :], 0.0),
                  reads=(), writes=("kho%d" % dc, "a1/%d" % dc), tag="khoz")
        KH_ALL = tuple("khe%d" % d for d in range(8)) + tuple("kho%d" % d for d in range(8))
        QH_TC = tuple("qh/%d/%d" % (tc, d) for d in range(8))
        for sub in range(4):
            for j in range(8):
                em.op("tensor", lambda eng, s_=sub, j_=j:
                      eng.transpose(pbbf(4)[:, 64 * j_:64 * j_ + 64],
                                    khat_e(j_)[0:64, 128 * s_:128 * s_ + 128], bfv(K_ident, 128)[0:64, 0:64]),
                      reads=KH_ALL + ("ident",), writes=("pb4",), tag="tpke")
                em.op("tensor", lambda eng, s_=sub, j_=j:
                      eng.transpose(pbbf(5)[:, 128 * j_:128 * j_ + 128],
                                    khat_o(j_)[:, 128 * s_:128 * s_ + 128], bfv(K_ident, 128)),
                      reads=KH_ALL + ("ident",), writes=("pb5",), tag="tpko")
            em.op("vector", lambda eng, s_=sub:
                  eng.tensor_copy(ktok(s_).rearrange("p (j c) -> p j c", j=8)[:, :, 0:64],
                                  pbbf(4)[:, 0:512].rearrange("p (j c) -> p j c", j=8)),
                  reads=("pb4",), writes=("kt%d" % sub,), tag="tpkev")
            em.op("vector", lambda eng, s_=sub:
                  eng.tensor_copy(ktok(s_).rearrange("p (j c) -> p j c", j=8)[:, :, 64:128],
                                  pbbf(5).rearrange("p (j c) -> p j c", j=8)[:, :, 64:128]),
                  reads=("pb5",), writes=("kt%d" % sub,), tag="tpkov")
        em.op("vector", lambda eng: eng.nop(),
              writes=tuple("ks%d" % d for d in range(8)) + ("scr1", "Asb_fence"), tag="fence")
        for sub in range(4):
            i = tc * 4 + sub
            for hg in range(4):
                bank = 4 + hg % 2
                for hh in range(4):
                    h = hg * 4 + hh
                    j = h // 2
                    if h % 2 == 0:
                        em.op("tensor", lambda eng, h_=h, j_=j, bk_=bank, hh_=hh, s_=sub, t_=tc:
                              eng.matmul(pb(bk_)[:, 128 * hh_:128 * hh_ + 128],
                                         khat_e(j_)[0:64, 128 * s_:128 * s_ + 128],
                                         Qh(j_)[0:64, t_ * TC + 128 * s_:t_ * TC + 128 * s_ + 128],
                                         start=True, stop=True),
                              reads=KH_ALL + QH_TC, writes=("pb%d" % bank,), tag="Amm")
                    else:
                        em.op("tensor", lambda eng, h_=h, j_=j, bk_=bank, hh_=hh, s_=sub, t_=tc:
                              eng.matmul(pb(bk_)[:, 128 * hh_:128 * hh_ + 128],
                                         khat_o(j_)[:, 128 * s_:128 * s_ + 128],
                                         Qh(j_)[:, t_ * TC + 128 * s_:t_ * TC + 128 * s_ + 128],
                                         start=True, stop=True),
                              reads=KH_ALL + QH_TC, writes=("pb%d" % bank,), tag="Amm")
                em.op("vector", lambda eng, hg_=hg, bk_=bank:
                      eng.tensor_tensor(out=Asb[:, 512 * hg_:512 * hg_ + 512].rearrange("p (r c) -> p r c", r=4),
                                        in0=pb(bk_).rearrange("p (r c) -> p r c", r=4),
                                        in1=bfv(K_caus, CH)[:, None, :].broadcast_to((128, 4, CH)),
                                        op=ALU.mult),
                      reads=("pb%d" % bank, "caus", "Asb_fence"), writes=("As%d" % hg,), tag="Amask")
            AS_ALL = tuple("As%d" % g for g in range(4))
            for h in range(H):
                bk, col = yslot(h)
                em.op("tensor", lambda eng, h_=h, bk_=bk, c_=col, s_=sub:
                      eng.matmul(pb(bk_)[0:64, c_:c_ + 128],
                                 vtok(s_)[:, 64 * h_:64 * h_ + 64],
                                 Asb[:, 128 * h_:128 * h_ + 128], start=True, stop=False),
                      reads=AS_ALL + ("v/%d" % sub,), writes=("pb%d" % bk,), tag="ymm")
                em.op("tensor", lambda eng, h_=h, bk_=bk, c_=col, s_=sub, t_=tc:
                      eng.matmul(pb(bk_)[0:64, c_:c_ + 128],
                                 Sbf[:, 64 * h_:64 * h_ + 64],
                                 Qh(h_ // 2)[:, t_ * TC + 128 * s_:t_ * TC + 128 * s_ + 128],
                                 start=False, stop=True),
                      reads=("SbfR",) + QH_TC, writes=("pb%d" % bk,), tag="ycr")
            for g in range(4):
                bk = YBANKS[g]
                par = g // 2
                t0 = 4 * (g % 2)
                em.op("vector", lambda eng, bk_=bk, p_=par, t0_=t0, i_=i:
                      eng.tensor_copy(bfv(Bs, 16384).rearrange("p (d t) -> p d t", d=8)
                                      [64 * p_:64 * p_ + 64, t0_:t0_ + 4, 128 * i_:128 * i_ + 128],
                                      pb(bk_)[0:64, :].rearrange("p (d t) -> p d t", d=4)),
                      reads=("pb%d" % bk,), writes=("Y/%d/%d" % (i, g),), tag="yev")
            for j in range(8):
                bk = 2 + j // 4
                em.op("tensor", lambda eng, j_=j, bk_=bk, s_=sub:
                      eng.matmul(pb(bk_)[:, 128 * (j_ % 4):128 * (j_ % 4) + 128],
                                 ktok(s_)[:, 128 * j_:128 * j_ + 128],
                                 vtok(s_)[:, 128 * j_:128 * j_ + 128], start=True, stop=True),
                      reads=("kt%d" % sub, "v/%d" % sub), writes=("pb%d" % bk,), tag="Smm")
            for bk in (2, 3):
                j0 = 4 * (bk - 2)
                em.op("vector", lambda eng, bk_=bk, j0_=j0:
                      eng.tensor_tensor(out=Srun[0:64, :].rearrange("p (j c) -> p j c", j=8)[:, j0_:j0_ + 4, :],
                                        in0=Srun[0:64, :].rearrange("p (j c) -> p j c", j=8)[:, j0_:j0_ + 4, :],
                                        in1=pb(bk_)[0:64, :].rearrange("p (j c) -> p j c", j=4)[:, :, 0:64],
                                        op=ALU.add),
                      reads=("pb%d" % bk, "Srun"), writes=("Srun",), tag="Sadd")
                em.op("vector", lambda eng, bk_=bk, j0_=j0:
                      eng.tensor_tensor(out=Srun[64:128, :].rearrange("p (j c) -> p j c", j=8)[:, j0_:j0_ + 4, :],
                                        in0=Srun[64:128, :].rearrange("p (j c) -> p j c", j=8)[:, j0_:j0_ + 4, :],
                                        in1=pb(bk_)[64:128, :].rearrange("p (j c) -> p j c", j=4)[:, :, 64:128],
                                        op=ALU.add),
                      reads=("pb%d" % bk, "Srun"), writes=("Srun",), tag="Sadd")
            em.op("vector", lambda eng: eng.tensor_tensor(out=Srun, in0=Srun, in1=bfv(K_Gc, 512), op=ALU.mult),
                  reads=("Srun", "Gc"), writes=("Srun",), tag="Sdec")
            em.op("vector", lambda eng:
                  eng.tensor_copy(Sbf[0:64, :].rearrange("p (j c) -> p j c", j=8)[:, :, 0:64],
                                  Srun[0:64, :].rearrange("p (j c) -> p j c", j=8)),
                  reads=("Srun",), writes=("SbfR",), tag="Sbfe")
            em.op("vector", lambda eng:
                  eng.tensor_copy(Sbf[64:128, :].rearrange("p (j c) -> p j c", j=8)[:, :, 64:128],
                                  Srun[64:128, :].rearrange("p (j c) -> p j c", j=8)),
                  reads=("Srun",), writes=("SbfR",), tag="Sbfo")

    em.mark("P4")
    em.barrier()

    em.op("vector", lambda eng: eng.tensor_copy(bfv(V_SccBf, 1024)[:, 0:512], Srun),
          reads=("Srun",), writes=("Scmp",), tag="Scmp")
    st_dma(cc3_in[:, :], bfv(V_SccBf, 1024)[:, 0:512], "cc3w", res_r=("Scmp",), tag="Sccw")
    em.op("gpsimd", lambda eng: eng.collective_compute(
          "AllGather", ALU.bypass, replica_groups=groups4,
          ins=[cc3_in.opt()], outs=[cc3_out.opt()]),
          reads=("cc3w",), writes=("cc3o",), cc=True, tag="SccAG")
    for j in range(4):
        ld(bfv(Ws + 8192 + 512 * j, 512), cc3_out[128 * j:128 * (j + 1), :], "Sg%d" % j,
           res_r=("cc3o",), tag="Sgld%d" % j)
    ld(bfv(Ws, 8192).rearrange("p (k c) -> p k c", k=8),
       wbf["owT"].rearrange("(k p) c -> p k c", p=128), "Wow", res_r=("hbm_owT", "Wq"))
    K_Wst = Ws + 8192 + 2048
    ld(bfv(K_Wst, 2048).rearrange("p (j c) -> p j c", j=4),
       Wst_i.rearrange("(j p) c -> p j c", p=128), "Wst", tag="ldWst")
    em.op("vector", lambda eng: eng.memset(fv(V_Scc, 512), 0.0), writes=("Scc",), tag="Scc0")
    for j in range(4):
        em.op("vector", lambda eng, j_=j: eng.tensor_tensor(out=fv(V_Srun, 512),
                                                            in0=bfv(Ws + 8192 + 512 * j_, 512),
                                                            in1=bfv(K_Wst + 512 * j_, 512), op=ALU.mult),
              reads=("Sg%d" % j, "Wst", "Srun"), writes=("Srun",), tag="Sgw%d" % j)
        em.op("vector", lambda eng: eng.tensor_tensor(out=fv(V_Scc, 512), in0=fv(V_Scc, 512),
                                                      in1=fv(V_Srun, 512), op=ALU.add),
              reads=("Srun", "Scc"), writes=("Scc",), tag="Sacc%d" % j)

    em.op("vector", lambda eng: eng.memset(bfv(V_SccBf, 1024), 0.0), writes=("SccBf",), tag="SccBfz")
    for i in range(NCH):
        tc = i // 4
        QH_TC = tuple("qh/%d/%d" % (tc, d) for d in range(8))
        em.op("vector", lambda eng:
              eng.tensor_copy(bfv(V_SccBf, 1024)[0:64, :].rearrange("p (j c) -> p j c", j=8)[:, :, 0:64],
                              fv(V_Scc, 512)[0:64, :].rearrange("p (j c) -> p j c", j=8)),
              reads=("Scc",), writes=("SccBf",), tag="fxbe")
        em.op("vector", lambda eng:
              eng.tensor_copy(bfv(V_SccBf, 1024)[64:128, :].rearrange("p (j c) -> p j c", j=8)[:, :, 64:128],
                              fv(V_Scc, 512)[64:128, :].rearrange("p (j c) -> p j c", j=8)),
              reads=("Scc",), writes=("SccBf",), tag="fxbo")
        for h in range(H):
            bk, col = yslot(h)
            em.op("tensor", lambda eng, h_=h, bk_=bk, c_=col, i_=i:
                  eng.matmul(pb(bk_)[0:64, c_:c_ + 128],
                             bfv(V_SccBf, 1024)[:, 64 * h_:64 * h_ + 64],
                             Qh(h_ // 2)[:, 128 * i_:128 * i_ + 128],
                             start=True, stop=True),
                  reads=("SccBf",) + QH_TC, writes=("pb%d" % bk,), tag="fxmm")
        for g in range(4):
            bk = YBANKS[g]
            par = g // 2
            t0 = 4 * (g % 2)
            em.op("vector", lambda eng, bk_=bk, p_=par, t0_=t0, i_=i:
                  eng.tensor_tensor(out=bfv(Bs, 16384).rearrange("p (d t) -> p d t", d=8)
                                    [64 * p_:64 * p_ + 64, t0_:t0_ + 4, 128 * i_:128 * i_ + 128],
                                    in0=bfv(Bs, 16384).rearrange("p (d t) -> p d t", d=8)
                                    [64 * p_:64 * p_ + 64, t0_:t0_ + 4, 128 * i_:128 * i_ + 128],
                                    in1=pb(bk_)[0:64, :].rearrange("p (d t) -> p d t", d=4), op=ALU.add),
                  reads=("pb%d" % bk, "Y/%d/%d" % (i, g)), writes=("Y/%d/%d" % (i, g),), tag="fxadd")
        if i < NCH - 1:
            em.op("vector", lambda eng: eng.tensor_tensor(out=fv(V_Scc, 512), in0=fv(V_Scc, 512),
                                                          in1=bfv(K_Gc, 512), op=ALU.mult),
                  reads=("Scc", "Gc"), writes=("Scc",), tag="fxdec")

    Y_ALL = tuple("Y/%d/%d" % (i, p) for i in range(NCH) for p in range(4))
    for tc in range(NTC):
        for dc in range(8):
            bank = dc % 2
            for k in range(8):
                em.op("tensor", lambda eng, k_=k, d_=dc, b_=bank, t_=tc:
                      eng.matmul(pb(b_), bfv(Ws, 8192)[:, 1024 * k_ + 128 * d_:1024 * k_ + 128 * d_ + 128],
                                 Yt(k_)[:, t_ * TC:(t_ + 1) * TC], start=(k_ == 0), stop=(k_ == 7)),
                      reads=("Wow",) + Y_ALL, writes=("pb%d" % bank,), tag="omm")
            em.op("vector", lambda eng, d_=dc, b_=bank, t_=tc:
                  eng.scalar_tensor_tensor(out=XN(d_)[:, t_ * TC:(t_ + 1) * TC], in0=pb(b_),
                                           scalar=fv(K_ob, 8)[:, d_:d_ + 1],
                                           in1=XN(d_)[:, t_ * TC:(t_ + 1) * TC],
                                           op0=ALU.add, op1=ALU.add),
                  reads=("pb%d" % bank, "obp", "XN/%d" % (tc * 4)) + XN_ALL,
                  writes=("X1/%d/%d" % (tc, dc),), tag="oev")

    X1_ALL = tuple("X1/%d/%d" % (t, d) for t in range(NTC) for d in range(8))
    em.mark("P4x")
    em.barrier()

    gn_stats_pre("g2", XN, X1_ALL, cc2_in, cc2_out)
    gn_stats_fin("g2", "fg", "fb", K_fg, K_fb, K_sc2, K_bs2)

    em.barrier()
    em.mark("P4b")

    ld(bfv(Bs, 16384).rearrange("p (k c) -> p k c", k=8),
       wbf["w1T"].rearrange("(k p) c -> p k c", p=128), "Ww1", res_r=("hbm_w1T",) + Y_ALL)
    ld(bfv(Cs, 16384).rearrange("p (k c) -> p k c", k=16),
       wbf["w2T"].rearrange("(k p) c -> p k c", p=128), "Ww2", res_r=("hbm_w2T",))

    def a2t(k):
        return bfv(Ds + 8192 + 512 * k, 512)

    def hsb(hc):
        return bfv(Ds + 512 * hc, 512)

    for tc in range(NTC):
        for k in range(8):
            em.op("vector", lambda eng, k_=k, t_=tc:
                  eng.tensor_scalar(out=a2t(k_), in0=XN(k_)[:, t_ * TC:(t_ + 1) * TC],
                                    scalar1=fv(K_sc2, 8)[:, k_:k_ + 1], scalar2=fv(K_bs2, 8)[:, k_:k_ + 1],
                                    op0=ALU.mult, op1=ALU.add),
                  reads=X1_ALL + ("g2sc", "g2bs"), writes=("a2/%d" % k,), tag="a2")
        A2_ALL = tuple("a2/%d" % k for k in range(8))
        for hc in range(16):
            bank = hc % 2
            for k in range(8):
                em.op("tensor", lambda eng, k_=k, h_=hc, b_=bank:
                      eng.matmul(pb(b_), bfv(Bs, 16384)[:, 2048 * k_ + 128 * h_:2048 * k_ + 128 * h_ + 128],
                                 a2t(k_), start=(k_ == 0), stop=(k_ == 7)),
                      reads=A2_ALL + ("Ww1",), writes=("pb%d" % bank,), tag="h1mm")
            em.op("scalar", lambda eng, h_=hc, b_=bank:
                  eng.activation(out=hsb(h_), in_=pb(b_), func=AF.Silu, bias=fv(K_b1, 16)[:, h_:h_ + 1]),
                  reads=("pb%d" % bank, "b1p"), writes=("hsb%d" % hc,), tag="h1ev")
        HSB_ALL = tuple("hsb%d" % h for h in range(16))
        for dc in range(8):
            bank = 2 + dc % 2
            for hk in range(16):
                em.op("tensor", lambda eng, k_=hk, d_=dc, b_=bank:
                      eng.matmul(pb(b_), bfv(Cs, 16384)[:, 1024 * k_ + 128 * d_:1024 * k_ + 128 * d_ + 128],
                                 hsb(k_), start=(k_ == 0), stop=(k_ == 15)),
                      reads=HSB_ALL + ("Ww2",), writes=("pb%d" % bank,), tag="h2mm")
            em.op("vector", lambda eng, d_=dc, b_=bank, t_=tc:
                  eng.scalar_tensor_tensor(out=XN(d_)[:, t_ * TC:(t_ + 1) * TC], in0=pb(b_),
                                           scalar=fv(K_b2, 8)[:, d_:d_ + 1],
                                           in1=XN(d_)[:, t_ * TC:(t_ + 1) * TC],
                                           op0=ALU.add, op1=ALU.add),
                  reads=("pb%d" % bank, "b2p") + X1_ALL,
                  writes=("S/%d/%d" % (tc, dc), "X1/%d/%d" % (tc, dc)), tag="sev")
    S_ALL = tuple("S/%d/%d" % (t, d) for t in range(NTC) for d in range(8))
    em.mark("P5")
    em.barrier()

    for tc in range(NTC):
        gt = bfv(Ds, 4096)
        sg = bfv(Ds + 4096, 4096)
        xt = fv(Ws, 4096)
        osb = fv(Us, 4096)
        ld(gt.rearrange("p (j t) -> p j t", j=8),
           g_spill.rearrange("(j p) t -> p j t", p=128)[:, :, tc * TC:(tc + 1) * TC],
           "gt%d" % (tc % 2), res_r=("gsp0", "gsp1"), tag="gld%d" % tc)
        ld(xt.rearrange("p (c d) -> p c d", c=4),
           x_in[tc * TC:(tc + 1) * TC, :].rearrange("(c p) d -> p c d", p=128),
           "xr%d" % (tc % 2), tag="xld%d" % tc)
        for j in range(8):
            em.op("vector", lambda eng, j_=j, t_=tc:
                  eng.tensor_tensor(out=sg[:, 512 * j_:512 * j_ + 512],
                                    in0=XN(j_)[:, t_ * TC:(t_ + 1) * TC],
                                    in1=gt[:, 512 * j_:512 * j_ + 512], op=ALU.mult),
                  reads=S_ALL + ("gt%d" % (tc % 2),), writes=("sg%d" % j,), tag="sg")
        SG_ALL = tuple("sg%d" % j for j in range(8))
        for sub in range(4):
            bank = sub % 2
            for j in range(8):
                em.op("tensor", lambda eng, j_=j, s_=sub, b_=bank:
                      eng.transpose(pbbf(b_)[:, 128 * j_:128 * j_ + 128],
                                    sg[:, 512 * j_ + 128 * s_:512 * j_ + 128 * s_ + 128],
                                    bfv(K_ident, 128)),
                      reads=SG_ALL + ("ident",), writes=("pb%d" % bank,), tag="tps")
            em.op("vector", lambda eng, s_=sub, b_=bank:
                  eng.tensor_tensor(out=osb[:, 1024 * s_:1024 * s_ + 1024],
                                    in0=xt[:, 1024 * s_:1024 * s_ + 1024],
                                    in1=pbbf(b_), op=ALU.add),
                  reads=("pb%d" % bank, "xr%d" % (tc % 2)), writes=("ot%d" % sub,), tag="oadd")
        st_dma(out_d[tc * TC:(tc + 1) * TC, :].rearrange("(c p) d -> p c d", p=128),
               osb.rearrange("p (c d) -> p c d", c=4), "out%d" % tc,
               res_r=("ot0", "ot1", "ot2", "ot3"), tag="stout%d" % tc)

    em.mark("P6")
    if upto is not None:
        em.truncate(upto)
        dd = nc.dram_tensor("dbg_pool", [128, POOL_N], BF16, kind="ExternalOutput").ap()
        DW = int(_osd.environ.get("KDUMPW", POOL_N))
        for seg in range(0, DW, 16384):
            w = min(16384, DW - seg)
            st_dma(dd[:, seg:seg + w], sbuf[:, seg:seg + w], "dbgpool%d" % seg,
                   res_r=tuple(em.res.keys()), tag="dbg")
        em.op("sync", lambda eng: eng.nop(),
              reads=tuple("dbgpool%d" % s for s in range(0, DW, 16384)), tag="fin")
    else:
        em.op("sync", lambda eng: eng.nop(), reads=tuple("out%d" % t for t in range(NTC)), tag="fin")

    em.finalize()
    em.check()
    import os as _os
    if _os.environ.get('KSTATS'):
        print('EMSTATS', em.stats())

    sems = {}
    for e in ENGINES:
        cm = nc.semaphore("sem_" + e); ctxs.append(cm); sems[("eng", e)] = cm.__enter__()
    for q in ("qsync", "qgp", "qact"):
        for i in range(DMA_SEM_K):
            cm = nc.semaphore("sem_%s%d" % (q, i)); ctxs.append(cm)
            sems[("dma", q, i)] = cm.__enter__()
    for i in range(em.n_cc):
        cm = nc.semaphore("sem_cc%d" % i); ctxs.append(cm); sems[("cc", i)] = cm.__enter__()
    blk_cm = nc.Block()
    ctxs.append(blk_cm)
    block = blk_cm.__enter__()
    em.emit(nc, block, sems)
    for cm in reversed(ctxs):
        cm.__exit__(None, None, None)
    return nc


def kernel(**inputs):
    x = np.asarray(inputs["L0_attn_x"], np.float32)
    cN = host_constants()

    def T(a):
        return np.ascontiguousarray(np.asarray(a, np.float32).T)

    base = {
        "qwT": T(inputs["L0_attn_qw"]), "kwT": T(inputs["L0_attn_kw"]),
        "vwT": T(inputs["L0_attn_vw"]), "owT": T(inputs["L0_attn_ow"]),
        "w1T": T(inputs["L0_attn_w1"]), "w2T": T(inputs["L0_attn_w2"]),
        "sW1": np.ascontiguousarray(np.asarray(inputs["L0_ffn_sW1"], np.float32)),
        "sW2": np.ascontiguousarray(np.asarray(inputs["L0_ffn_sW2"], np.float32)),
        "rW1": np.ascontiguousarray(np.asarray(inputs["L0_ffn_rW1"], np.float32)),
        "rW2": np.ascontiguousarray(np.asarray(inputs["L0_ffn_rW2"], np.float32)),
        "RWT": T(inputs["L0_ffn_routing_W"]),
        "rb2": np.ascontiguousarray(np.asarray(inputs["L0_ffn_rb2"], np.float32)),
        "ng_pc": pc_layout(inputs["L0_attn_norm_g"], D),
        "nb_pc": pc_layout(inputs["L0_attn_norm_b"], D),
        "fg_pc": pc_layout(inputs["L0_attn_ffn_norm_g"], D),
        "fb_pc": pc_layout(inputs["L0_attn_ffn_norm_b"], D),
        "ob_pc": pc_layout(inputs["L0_attn_ob"], D),
        "b2_pc": pc_layout(inputs["L0_attn_b2"], D),
        "sb2_pc": pc_layout(inputs["L0_ffn_sb2"], D),
        "b1_pc": pc_layout(inputs["L0_attn_b1"], DFF),
        "sb1_pc": pc_layout(inputs["L0_ffn_sb1"], EH),
        "rb1_pc": np.ascontiguousarray(np.stack([pc_layout(inputs["L0_ffn_rb1"][e], EH) for e in range(8)])),
        "eb_pc": np.ascontiguousarray(np.asarray(inputs["L0_ffn_expert_bias"], np.float32).reshape(8, 1)),
    }
    base.update(cN)

    in_maps = []
    for core in range(NCORE):
        b = core // 4
        t0 = (core % 4) * TOK
        m = dict(base)
        m["x_in"] = np.ascontiguousarray(x[b, t0:t0 + TOK, :])
        m.update(core_constants(core))
        in_maps.append(m)

    nc = build_program()
    res = run_bass_kernel_spmd(nc, in_maps, core_ids=list(range(NCORE)))
    out = np.empty((B_, T_, D), np.float32)
    for core in range(NCORE):
        b = core // 4
        t0 = (core % 4) * TOK
        out[b, t0:t0 + TOK, :] = res.results[core]["out"]
    return out
